# version 61 (final)
# speedup vs baseline: 1.0477x; 1.0423x over previous
import numpy as np
import ml_dtypes

import concourse.bass as bass
import concourse.mybir as mybir
from concourse.bass_utils import run_bass_kernel_spmd

F32 = mybir.dt.float32
BF16 = mybir.dt.bfloat16
AF = mybir.ActivationFunctionType
ALU = mybir.AluOpType
AXX = mybir.AxisListType.X

H = 16
DM = 1024
KCN = 8
TOK = 2048
NCH = 16
NG = 4
CH = 128
GT = 512
PHI = 153
PPAD = 192
NPC = (H * PPAD) // 128
PHIQ_COLS = H * PPAD
PHIK_COLS = H * PHI
QKP = 544
VA = H * 65
WQKV_COLS = QKP + QKP + VA
FF = 4096
SC_SCORE = float(1.0 / np.sqrt(32.0))
BI_SCORE = float(1.0 / np.sqrt(2.0))
SC_DIAGK = float(1.0 / np.sqrt(32.0))
EPS_LN = 1e-5
EPS_DEN = 1e-12
INV_DM = 1.0 / DM

GAP_OFF = {}
_o = 33
for _g in range(1, 16):
    GAP_OFF[_g] = _o
    _o += 16 - _g
assert _o == PHI

PARTS = []
for _h in range(H):
    _s = PPAD * _h
    _c0, _b0 = _s // 128, _s % 128
    assert _b0 in (0, 64)
    _r0 = min(128 - _b0, PHI)
    PARTS.append((_h, _c0, _b0, _r0, 0))
    PARTS.append((_h, _c0 + 1, 0, PHI - _r0, _r0))

BD = [[] for _ in range(NPC)]
for _p in PARTS:
    BD[_p[1]].append(_p)

TILE_W = [65 * len(BD[c]) for c in range(NPC)]
STATE_LOC = []
_bank, _col = 0, 0
for _c in range(NPC):
    if _col + TILE_W[_c] > 512:
        _bank += 1
        _col = 0
    STATE_LOC.append((_bank, _col))
    _col += TILE_W[_c]
N_STATE_BANKS = _bank + 1
assert N_STATE_BANKS == 5


def y_loc(h):
    if h < 7:
        return 0, 65 * h
    if h < 14:
        return 1, 65 * (h - 7)
    return 2, 65 * (h - 14)


SCORE_COL = [140, 268]


def fm32_loc(h):
    j, r = divmod(h, 4)
    return (j, 32 * r)


FM32_SRC = [128 * t for t in range(4)]

PIECES = [(0, 512), (512, 512), (1024, 512), (1536, 512), (2048, 80)]
PIECE_EVAC = [
    [("q", 0, 0, 512)],
    [("q", 0, 512, 32), ("k", 32, 0, 480)],
    [("k", 0, 480, 64), ("v", 64, 0, 448)],
    [("v", 0, 448, 512)],
    [("v", 0, 960, 80)],
]


def build_nc(stage=None):
    nc = bass.Bass("TRN2", target_bir_lowering=False, debug=False)
    stage_snaps = {}

    d_xt = nc.dram_tensor("xt", [NG * KCN * 128, GT], F32, kind="ExternalInput").ap()
    d_xp = nc.dram_tensor("xp", [NG * KCN * 128, GT], F32, kind="ExternalInput").ap()
    d_gam = nc.dram_tensor("gam", [128, 1], F32, kind="ExternalInput").ap()
    d_wqkv = nc.dram_tensor("wqkv", [DM, WQKV_COLS], F32, kind="ExternalInput").ap()
    d_wot = nc.dram_tensor("wot", [DM, DM], F32, kind="ExternalInput").ap()
    d_wit = nc.dram_tensor("wit", [DM, FF], F32, kind="ExternalInput").ap()
    d_wot2 = nc.dram_tensor("wot2", [FF, DM], F32, kind="ExternalInput").ap()
    d_wgt = nc.dram_tensor("wgt", [DM, 8], F32, kind="ExternalInput").ap()
    d_lat = nc.dram_tensor("lat", [DM, 32], F32, kind="ExternalInput").ap()
    d_lbt = nc.dram_tensor("lbt", [32, FF], F32, kind="ExternalInput").ap()
    d_wib = nc.dram_tensor("wib", [128, 32], F32, kind="ExternalInput").ap()
    d_wob = nc.dram_tensor("wob", [128, 8], F32, kind="ExternalInput").ap()
    d_gbr = nc.dram_tensor("gbr", [1, 8], F32, kind="ExternalInput").ap()
    d_mask = nc.dram_tensor("maskt", [128, 128], F32, kind="ExternalInput").ap()
    d_ident = nc.dram_tensor("identt", [128, 128], F32, kind="ExternalInput").ap()
    d_out = nc.dram_tensor("outt", [NG * KCN * 128, GT], F32,
                           kind="ExternalOutput").ap()


    d_qkpad = nc.dram_tensor("qkpad", [TOK, 2 * QKP], BF16, kind="Internal").ap()
    d_phiq = nc.dram_tensor("phiqd", [TOK, PHIQ_COLS], BF16, kind="Internal").ap()
    d_ydiv = nc.dram_tensor("ydivd", [TOK, DM], BF16, kind="Internal").ap()
    d_x2 = nc.dram_tensor("x2d", [NG * KCN * 128, GT], F32, kind="Internal").ap()

    ENGS = ("sync", "tensor", "vector", "scalar", "gpsimd")
    ops = {e: [] for e in ENGS}
    cnt = {}
    last_wait = {e: {} for e in ENGS}

    def op(eng, fn, waits=(), inc=None):
        for (s, v) in waits:
            if v <= 0 or last_wait[eng].get(s, -1) >= v:
                continue
            last_wait[eng][s] = v
            ops[eng].append(("wait", s, v))
        if fn is None:
            return
        if inc is not None:
            s, amt = inc
            cnt[s] = cnt.get(s, 0) + amt
            ops[eng].append(("do", fn, s, amt))
        else:
            ops[eng].append(("do", fn, None, 0))

    def mark(s):
        return cnt.get(s, 0)

    def snap_stage(name):
        stage_snaps[name] = {e: len(ops[e]) for e in ENGS}

    DMA_ISSUER = {"ld": "sync", "lb": "sync", "st": "sync", "tr": "sync",
                  "t2": "sync", "go": "gpsimd"}

    def dma_mark(s):
        v = cnt.get(s, 0)
        op(DMA_ISSUER[s], None, waits=[(s, v)])
        return v

    lazy_settle = {}

    def batch_begin(s):
        op(DMA_ISSUER[s], None, waits=[(s, lazy_settle.get(s, 0))])

    def batch_end(s):
        lazy_settle[s] = cnt.get(s, 0)
        return lazy_settle[s]

    import contextlib
    with contextlib.ExitStack() as _stk:
        def S(name, shape, dt):
            return _stk.enter_context(nc.sbuf_tensor(name, shape, dt))
        def P(name, shape, dt=F32):
            return _stk.enter_context(nc.psum_tensor(name, shape, dt))
        wgt_sb = S("wgt_sb", [128, KCN * 8], BF16)
        lat_sb = S("lat_sb", [128, KCN * 32], BF16)
        wib_sb = S("wib_sb", [128, 32], F32)
        wob_sb = S("wob_sb", [128, 8], F32)
        gbr_sb = S("gbr_sb", [1, 8], BF16)
        ident_sb = S("ident_sb", [128, 128], BF16)
        onec = S("onec", [128, 1], BF16)
        onecf = S("onecf", [128, 1], F32)
        onerow = S("onerow", [1, 128], BF16)
        srow = S("srow", [1, 3 * GT], F32)
        arow = S("arow", [1, GT], F32)
        ps_g0 = P("ps_g0", [128, 512])
        ps_g1 = P("ps_g1", [128, 512])
        ps_g2 = P("ps_g2", [128, 512])
        SEM_ld = _stk.enter_context(nc.semaphore("ld"))
        SEM_lb = _stk.enter_context(nc.semaphore("lb"))
        SEM_tr = _stk.enter_context(nc.semaphore("tr"))
        SEM_t2 = _stk.enter_context(nc.semaphore("t2"))
        SEM_go = _stk.enter_context(nc.semaphore("go"))
        SEM_st = _stk.enter_context(nc.semaphore("st"))
        SEM_pe = _stk.enter_context(nc.semaphore("pe"))
        SEM_ve = _stk.enter_context(nc.semaphore("ve"))
        SEM_ac = _stk.enter_context(nc.semaphore("ac"))
        SEM_po = _stk.enter_context(nc.semaphore("po"))
        block = _stk.enter_context(nc.Block())
        SEM = {"ld": SEM_ld, "lb": SEM_lb, "tr": SEM_tr, "t2": SEM_t2,
               "go": SEM_go, "st": SEM_st, "pe": SEM_pe, "ve": SEM_ve,
               "ac": SEM_ac, "po": SEM_po}
        PS_G = [ps_g0, ps_g1, ps_g2]

        mixer_end_marks = {}

        with contextlib.ExitStack() as _mstk:
            def MS(name, shape, dt):
                return _mstk.enter_context(nc.sbuf_tensor(name, shape, dt))
            ps_s0 = _mstk.enter_context(nc.psum_tensor("ps_s0", [128, 512], F32))
            ps_s1 = _mstk.enter_context(nc.psum_tensor("ps_s1", [128, 512], F32))
            ps_s2 = _mstk.enter_context(nc.psum_tensor("ps_s2", [128, 512], F32))
            ps_s3 = _mstk.enter_context(nc.psum_tensor("ps_s3", [128, 512], F32))
            ps_s4 = _mstk.enter_context(nc.psum_tensor("ps_s4", [128, 512], F32))
            wqkv_sb = MS("wqkv_sb", [128, KCN * WQKV_COLS], BF16)
            xa_sb = MS("xa_sb", [128, KCN * GT], BF16)
            qkpad_sb = MS("qkpad_sb", [128, 2 * 2 * QKP], BF16)
            vaug_sb = MS("vaug_sb", [128, 4 * VA], BF16)
            k4_sb = MS("k4_sb", [128, QKP], BF16)
            phik_sb = MS("phik_sb", [128, 4 * PHIK_COLS], BF16)
            phiq_sb = MS("phiq_sb", [128, 2 * PHIQ_COLS], BF16)
            fm32_sb = MS("fm32_sb", [128, 16 * GT], BF16)
            phiqfm_sb = MS("phiqfm_sb", [128, NPC * GT], BF16)
            snap_sb = MS("snap_sb", [128, N_STATE_BANKS * 512], BF16)
            sqtmp = MS("sqtmp", [128, 2 * 128], F32)
            mst_sb = MS("mst_sb", [128, 16 * 128], BF16)
            ydiv_sb = MS("ydiv_sb", [128, 2 * DM], BF16)
            rec_sb = MS("rec_sb", [128, 48], F32)
            wot_sb = MS("wot_sb", [128, KCN * DM], BF16)
            mask_sb = MS("mask_sb", [128, 128], F32)
            gam_sb = MS("gam_sb", [128, 1], F32)
            zt = MS("zt", [128, 512], BF16)
            scb = MS("scb", [128, 2], F32)
            onerowf = MS("onerowf", [1, 128], F32)
            yT_sb = MS("yT_sb", [128, 8 * GT], BF16)
            xg_sb = MS("xg_sb", [128, KCN * GT], F32)
            sqs_sb = MS("sqs_sb", [128, 2 * GT], BF16)
            acol_sb = MS("acol_sb", [128, 12], F32)
            res_sb = MS("res_sb", [128, 4 * GT], F32)
            PS_STATE = [ps_s0, ps_s1, ps_s2, ps_s3, ps_s4]

            op("sync", lambda: nc.sync.dma_start(out=mask_sb[:], in_=d_mask), inc=("ld", 16))
            op("sync", lambda: nc.sync.dma_start(out=wib_sb[:], in_=d_wib), inc=("ld", 16))
            op("sync", lambda: nc.sync.dma_start(out=wob_sb[:], in_=d_wob), inc=("ld", 16))
            op("sync", lambda: nc.sync.dma_start(out=gam_sb[:], in_=d_gam), inc=("ld", 16))
            op("gpsimd", lambda: nc.gpsimd.dma_start(out=ident_sb[:], in_=d_ident), inc=("go", 16))
            op("gpsimd", lambda: nc.gpsimd.dma_start(out=gbr_sb[:], in_=d_gbr), inc=("go", 16))
            op("gpsimd", lambda: nc.gpsimd.dma_start(
                out=wqkv_sb[:].rearrange("p (c n) -> p c n", n=WQKV_COLS)[:, :, QKP:],
                in_=d_wqkv.rearrange("(c p) n -> p c n", p=128)[:, :, QKP:]),
                inc=("go", 16))
            go_kv = dma_mark("go")
            op("gpsimd", lambda: nc.gpsimd.dma_start(
                out=wqkv_sb[:].rearrange("p (c n) -> p c n", n=WQKV_COLS)[:, :, 0:QKP],
                in_=d_wqkv.rearrange("(c p) n -> p c n", p=128)[:, :, 0:QKP]),
                inc=("go", 16))
            op("gpsimd", lambda: nc.gpsimd.dma_start(
                out=wgt_sb[:].rearrange("p (c n) -> p c n", n=8),
                in_=d_wgt.rearrange("(c p) n -> p c n", p=128)), inc=("go", 16))
            op("gpsimd", lambda: nc.gpsimd.dma_start(
                out=lat_sb[:].rearrange("p (c n) -> p c n", n=32),
                in_=d_lat.rearrange("(c p) n -> p c n", p=128)), inc=("go", 16))
            go_smalls = dma_mark("go")
            ld_smalls = dma_mark("ld")

            op("vector", lambda: nc.vector.memset(zt[:], 0.0), inc=("ve", 1))
            op("vector", lambda: nc.vector.memset(onecf[:], 1.0), inc=("ve", 1))
            op("vector", lambda: nc.vector.memset(onerowf[:], 1.0), inc=("ve", 1))
            op("vector", lambda: nc.vector.memset(onec[:], 1.0), inc=("ve", 1))
            op("vector", lambda: nc.vector.memset(onerow[:], 1.0), inc=("ve", 1))
            op("vector", lambda: nc.vector.memset(scb[:, 0:1], SC_SCORE), inc=("ve", 1))
            op("vector", lambda: nc.vector.memset(scb[:, 1:2], BI_SCORE), inc=("ve", 1))
            for s in range(2):
                b0 = s * PHIQ_COLS
                op("vector", (lambda b0=b0: nc.vector.memset(
                    phiq_sb[:, b0:b0 + PHIQ_COLS], 0.0)), inc=("ve", 1))
                op("vector", (lambda b0=b0: nc.vector.memset(
                    bass.AP(phiq_sb[:].tensor, b0,
                            [[2 * PHIQ_COLS, 128], [PPAD, H], [1, 1]]), 1.0)),
                    waits=[("ve", mark("ve"))], inc=("ve", 1))
            for s in range(4):
                b0 = s * PHIK_COLS
                op("vector", (lambda b0=b0: nc.vector.memset(
                    bass.AP(phik_sb[:].tensor, b0,
                            [[4 * PHIK_COLS, 128], [PHI, H], [1, 1]]), 1.0)), inc=("ve", 1))
            ve_const = mark("ve")

            def emit_wsum(src_sb, total, stride, dst_row):
                for p0 in range(0, total, 512):
                    pw = min(512, total - p0)
                    vw = mark("ve")
                    for k in range(KCN):
                        op("tensor", (lambda k=k, p0=p0, pw=pw, src_sb=src_sb, stride=stride:
                            nc.tensor.matmul(ps_g2[0:1, 0:pw], onec[:, 0:1],
                                             src_sb[:, k * stride + p0:k * stride + p0 + pw],
                                             start=(k == 0), stop=(k == KCN - 1))),
                            waits=[("go", go_smalls), ("ve", max(ve_const, vw))],
                            inc=(("pe", 1) if k == KCN - 1 else None))
                    pv = mark("pe")
                    op("vector", (lambda p0=p0, pw=pw, dst_row=dst_row:
                        nc.vector.tensor_copy(dst_row[0:1, p0:p0 + pw], ps_g2[0:1, 0:pw])),
                        waits=[("pe", pv)], inc=("ve", 1))



            for b in range(N_STATE_BANKS):
                op("tensor", (lambda b=b: nc.tensor.matmul(
                    PS_STATE[b][:, 0:512], zt[:, 0:128], zt[:, 0:512],
                    start=True, stop=True)), waits=[("ve", ve_const)], inc=("pe", 1))
            snap_stage("prologue")
            snap_ready = {"done": 0}
            snap_ack = [("ac", 0)] * N_STATE_BANKS
            upd_done = {"pe": mark("pe")}
            PARTS_BY_BANK = sorted(PARTS, key=lambda p: STATE_LOC[p[1]][0])

            qk_slot_free = [{} for _ in range(2)]
            phiq_slot_free = [{} for _ in range(2)]
            vk_slot_free = [{} for _ in range(4)]
            ydiv_slot_free = [{} for _ in range(2)]
            res_slot_free = [{} for _ in range(4)]
            xstage_free = {}
            scores_free = {}
            yev_done = {"ve": 0}
            fm32_free = {}
            xg_readers = [dict() for _ in range(KCN)]

            def issue_x_loads(xsrc, gidx, skip=()):
                t0 = GT * gidx
                for k in range(KCN):
                    if k in skip:
                        continue
                    s = "ld" if k % 2 == 0 else "lb"
                    q = "sync" if k % 2 == 0 else "scalar"
                    eng = nc.sync if k % 2 == 0 else nc.scalar
                    wl = [(ss, vv) for ss, vv in xg_readers[k].items()]
                    op(q, (lambda k=k, gidx=gidx, xsrc=xsrc, eng=eng:
                        eng.dma_start(
                            out=xg_sb[:, k * GT:(k + 1) * GT],
                            in_=xsrc[(gidx * KCN + k) * 128:
                                     (gidx * KCN + k + 1) * 128, :])),
                        waits=wl, inc=(s, 16))
                return {"ld": mark("ld"), "lb": mark("lb")}

            def issue_one_x_load(xsrc, gidx, k, extra_waits):
                t0 = GT * gidx
                s = "ld" if k % 2 == 0 else "lb"
                wl = [(ss, vv) for ss, vv in xg_readers[k].items()]
                op("scalar", (lambda k=k, gidx=gidx, xsrc=xsrc:
                    nc.scalar.dma_start(
                        out=xg_sb[:, k * GT:(k + 1) * GT],
                        in_=xsrc[(gidx * KCN + k) * 128:
                                 (gidx * KCN + k + 1) * 128, :])),
                    waits=wl + extra_waits, inc=(s, 16))

            stats_marks = {"acol": ("ve", 0),
                           "agam": ("ve", 0), "xa": ("ve", 0),
                           "ps0free": ("ve", 0)}
            sqs_war = {"pe": 0}

            def emit_stats_xa(load_marks):
                entry_ve, entry_po = mark("ve"), mark("po")
                entry_ac = mark("ac")
                lw = [("ld", load_marks["ld"]), ("lb", load_marks["lb"])]
                xw_pe = xstage_free.get("pe", 0)
                for k in range(KCN):
                    op("tensor", (lambda k=k: nc.tensor.matmul(
                        ps_g2[0:1, 0:GT], onecf[:, 0:1], xg_sb[:, k * GT:(k + 1) * GT],
                        start=(k == 0), stop=(k == KCN - 1))),
                        waits=(lw + [("ac", entry_ac), ("ve", entry_ve),
                                     ("po", entry_po)]) if k == 0 else (),
                        inc=(("pe", 1) if k == KCN - 1 else None))
                pe_sum = mark("pe")
                op("vector", lambda: nc.vector.tensor_scalar(
                    srow[0:1, 0:GT], ps_g2[0:1, 0:GT], INV_DM, None, ALU.mult),
                    waits=[("pe", pe_sum)], inc=("ve", 1))
                op("vector", lambda: nc.vector.tensor_scalar(
                    arow[0:1, :], srow[0:1, 0:GT], -1.0, None, ALU.mult),
                    waits=[("ve", mark("ve"))], inc=("ve", 1))
                ve_negm = mark("ve")
                bw1 = [(s, v) for s, v in proj_evac_done[1].items()]
                op("tensor", lambda: nc.tensor.matmul(
                    ps_g1[:, 0:GT], onerowf[0:1, :], arow[0:1, :],
                    start=True, stop=True),
                    waits=[("ve", ve_negm)] + bw1, inc=("pe", 1))
                pe_mbc = mark("pe")
                cast_marks = []
                for k in range(KCN):
                    op("vector", (lambda k=k: nc.vector.tensor_add(
                        xa_sb[:, k * GT:(k + 1) * GT],
                        xg_sb[:, k * GT:(k + 1) * GT],
                        ps_g1[:, 0:GT])),
                        waits=([("pe", pe_mbc), ("pe", xw_pe)]
                               if k == 0 else ()),
                        inc=("ve", 1))
                    cast_marks.append(("ve", mark("ve")))
                stats_marks["xa"] = ("ve", mark("ve"))
                sq_pe = {-2: sqs_war["pe"], -1: sqs_war["pe"]}
                for k in range(KCN):
                    op("scalar", (lambda k=k: nc.scalar.activation(
                        out=sqs_sb[:, (k % 2) * GT:(k % 2 + 1) * GT],
                        in_=xa_sb[:, k * GT:(k + 1) * GT],
                        func=AF.Square)),
                        waits=[cast_marks[k], ("pe", sq_pe[k - 2])], inc=("ac", 1))
                    sqm = ("ac", mark("ac"))
                    op("tensor", (lambda k=k: nc.tensor.matmul(
                        ps_g0[0:1, 0:GT], onec[:, 0:1],
                        sqs_sb[:, (k % 2) * GT:(k % 2 + 1) * GT],
                        start=(k == 0), stop=(k == KCN - 1))),
                        waits=[sqm], inc=("pe", 1))
                    sq_pe[k] = mark("pe")
                sqs_war["pe"] = sq_pe[KCN - 1]
                pv = mark("pe")
                op("vector", lambda: nc.vector.tensor_scalar(
                    srow[0:1, 2 * GT:3 * GT], ps_g0[0:1, 0:GT], INV_DM, None,
                    ALU.mult), waits=[("pe", pv)], inc=("ve", 1))
                op("vector", lambda: nc.vector.tensor_scalar(
                    srow[0:1, 2 * GT:3 * GT], srow[0:1, 2 * GT:3 * GT], EPS_LN,
                    None, ALU.add), waits=[("ve", mark("ve"))], inc=("ve", 1))
                stats_marks["ps0free"] = ("ve", mark("ve"))
                vv = mark("ve")
                for ci in range(4):
                    op("tensor", (lambda ci=ci: nc.tensor.matmul(
                        ps_g2[:, 8 + ci:9 + ci],
                        srow[0:1, 2 * GT + ci * CH:2 * GT + (ci + 1) * CH],
                        onecf[0:1, 0:1], start=True, stop=True,
                        skip_group_check=True)),
                        waits=([("ve", vv)] if ci == 0 else ()),
                        inc=(("pe", 1) if ci == 3 else None))
                pv2 = mark("pe")
                op("vector", lambda: nc.vector.tensor_copy(
                    acol_sb[:, 0:4], ps_g2[:, 8:12]),
                    waits=[("pe", pv2), ("ve", entry_ve), ("po", entry_po)],
                    inc=("ve", 1))
                vcp = mark("ve")
                op("scalar", lambda: nc.scalar.activation(
                    out=acol_sb[:, 4:8], in_=acol_sb[:, 0:4], func=AF.Ln),
                    waits=[("ve", vcp)], inc=("ac", 1))
                op("scalar", lambda: nc.scalar.activation(
                    out=acol_sb[:, 0:4], in_=acol_sb[:, 4:8], func=AF.Exp,
                    scale=-0.5), waits=[("ac", mark("ac"))], inc=("ac", 1))
                ac_a = mark("ac")
                stats_marks["acol"] = ("ac", ac_a)
                op("vector", lambda: nc.vector.tensor_scalar(
                    acol_sb[:, 8:12], acol_sb[:, 0:4], gam_sb[:, 0:1], None,
                    ALU.mult),
                    waits=[("ac", ac_a), ("ld", ld_smalls)], inc=("ve", 1))
                stats_marks["agam"] = ("ve", mark("ve"))
                endm = {"ve": mark("ve"), "po": mark("po"), "pe": pe_sum}
                for k in range(KCN):
                    xg_readers[k] = dict(endm)
                return {"ve": mark("ve"), "po": mark("po"), "ac": mark("ac"),
                        "pe": mark("pe")}

            def emit_proj_chunk(gidx, ci, prep, upd_q=None):
                cglob = gidx * 4 + ci
                qs = cglob % 2
                vs = cglob % 4
                xa_ready = dict([stats_marks["xa"], stats_marks["ps0free"]])
                piece_list = range(1, 5) if prep else range(5)
                pebank = 0
                for pi in piece_list:
                    p0, pw = PIECES[pi]
                    bank = pebank % 2
                    psb = PS_G[bank]
                    pebank += 1
                    evw = dict(proj_evac_done[bank])
                    for k in range(KCN):
                        op("tensor", (lambda k=k, p0=p0, pw=pw, psb=psb, ci=ci:
                            nc.tensor.matmul(
                                psb[:, 0:pw],
                                xa_sb[:, k * GT + ci * CH:k * GT + ci * CH + CH],
                                wqkv_sb[:, k * WQKV_COLS + p0:k * WQKV_COLS + p0 + pw],
                                start=(k == 0), stop=(k == KCN - 1))),
                            waits=([(s, v) for s, v in xa_ready.items()]
                                   + [(s, v) for s, v in evw.items()]
                                   + [("go", go_kv if prep else go_smalls)]),
                            inc=(("pe", 1) if k == KCN - 1 else None))
                    pv = mark("pe")
                    if upd_q is not None:
                        for _ in range(8):
                            if upd_q:
                                upd_q.pop(0)()
                    for (dst, po_, do_, w_) in PIECE_EVAC[pi]:
                        if prep and dst == "q":
                            continue
                        if dst == "q":
                            dap = qkpad_sb[:, qs * 2 * QKP + do_:qs * 2 * QKP + do_ + w_]
                        elif dst == "k":
                            dap = qkpad_sb[:, qs * 2 * QKP + QKP + do_:qs * 2 * QKP + QKP + do_ + w_]
                        else:
                            dap = vaug_sb[:, vs * VA + do_:vs * VA + do_ + w_]
                        slot_waits = [(s, v) for s, v in
                                      (qk_slot_free[qs] if dst in "qk" else vk_slot_free[vs]).items()]
                        if prep and dst == "v":
                            acp, aw = 8 + ci, stats_marks["agam"]
                        else:
                            acp, aw = ci, stats_marks["acol"]
                        op("vector", (lambda dap=dap, psb=psb, po_=po_, w_=w_,
                                      acp=acp: nc.vector.tensor_scalar(
                                dap, psb[:, po_:po_ + w_],
                                acol_sb[:, acp:acp + 1], None, ALU.mult)),
                            waits=[("pe", pv), aw] + slot_waits,
                            inc=("ve", 1))
                    proj_evac_done[bank] = {"ve": mark("ve")}
                ones_ap = bass.AP(vaug_sb[:].tensor, vs * VA + 64, [[4 * VA, 128], [65, H], [1, 1]])
                if prep:
                    gsrc = bass.AP(gam_sb[:].tensor, 0, [[1, 128], [0, H], [0, 1]])
                    op("vector", (lambda ones_ap=ones_ap, gsrc=gsrc:
                        nc.vector.tensor_copy(ones_ap, gsrc)),
                        waits=[("ve", mark("ve"))], inc=("ve", 1))
                else:
                    op("vector", (lambda ones_ap=ones_ap: nc.vector.memset(ones_ap, 1.0)),
                        waits=[("ve", mark("ve"))], inc=("ve", 1))
                if not prep:
                    op("sync", (lambda qs=qs, cglob=cglob: nc.sync.dma_start(
                        out=d_qkpad[cglob * CH:(cglob + 1) * CH, :],
                        in_=qkpad_sb[:, qs * 2 * QKP:(qs + 1) * 2 * QKP])),
                        waits=[("ve", mark("ve"))], inc=("st", 16))
                    proj_qk_st["st"] = dma_mark("st")
                return qs, vs

            proj_evac_done = {0: {}, 1: {}}
            proj_qk_st = {"st": 0}

            def emit_phi_chunk(gidx, ci, prep):
                cglob = gidx * 4 + ci
                qs = cglob % 2
                vs = cglob % 4
                qbase = qs * 2 * QKP
                kbase = qbase + QKP
                ve_evac = mark("ve")
                op("vector", lambda: nc.vector.tensor_scalar(
                    k4_sb[:, 0:QKP], qkpad_sb[:, kbase:kbase + QKP], 0.25, None, ALU.mult),
                    waits=[("ve", ve_evac)], inc=("ve", 1))
                ve_k4 = mark("ve")
                pkb = vs * PHIK_COLS
                slot_waits = [(s, v) for s, v in vk_slot_free[vs].items()]
                op("vector", (lambda pkb=pkb, kbase=kbase: nc.vector.tensor_scalar(
                    bass.AP(phik_sb[:].tensor, pkb + 1,
                            [[4 * PHIK_COLS, 128], [PHI, H], [1, 16]]),
                    bass.AP(qkpad_sb[:].tensor, kbase,
                            [[2 * 2 * QKP, 128], [32, H], [1, 16]]),
                    0.25, None, ALU.mult)),
                    waits=[("ve", ve_evac)] + slot_waits, inc=("ve", 1))
                op("scalar", (lambda pkb=pkb, kbase=kbase: nc.scalar.activation(
                    out=bass.AP(phik_sb[:].tensor, pkb + 17,
                                [[4 * PHIK_COLS, 128], [PHI, H], [1, 16]]),
                    in_=bass.AP(qkpad_sb[:].tensor, kbase,
                                [[2 * 2 * QKP, 128], [32, H], [1, 16]]),
                    func=AF.Square, scale=scb[:, 0:1])),
                    waits=[("ve", ve_evac)] + slot_waits, inc=("ac", 1))
                for gp in range(1, 16):
                    w = 16 - gp
                    dst = bass.AP(phik_sb[:].tensor, pkb + GAP_OFF[gp],
                                  [[4 * PHIK_COLS, 128], [PHI, H], [1, w]])
                    if gp % 2:
                        in0 = bass.AP(qkpad_sb[:].tensor, kbase,
                                      [[2 * 2 * QKP, 128], [32, H], [1, w]])
                        in1 = bass.AP(qkpad_sb[:].tensor, kbase + gp,
                                      [[2 * 2 * QKP, 128], [32, H], [1, w]])
                        op("vector", (lambda dst=dst, in0=in0, in1=in1:
                            nc.vector.scalar_tensor_tensor(
                                dst, in0, 0.0625, in1, ALU.mult, ALU.mult)),
                            waits=[("ve", ve_evac)] + slot_waits, inc=("ve", 1))
                    else:
                        in0 = bass.AP(k4_sb[:].tensor, 0, [[QKP, 128], [32, H], [1, w]])
                        in1 = bass.AP(k4_sb[:].tensor, gp, [[QKP, 128], [32, H], [1, w]])
                        op("gpsimd", (lambda dst=dst, in0=in0, in1=in1:
                            nc.gpsimd.tensor_tensor(dst, in0, in1, ALU.mult)),
                            waits=[("ve", ve_k4)] + slot_waits, inc=("po", 1))
                marks_k = {"ve": mark("ve"), "po": mark("po"), "ac": mark("ac")}
                if prep:
                    qk_slot_free[qs] = dict(marks_k)
                    return marks_k
                ps_ = cglob % 2
                pqb = ps_ * PHIQ_COLS
                qslot_waits = [(s, v) for s, v in phiq_slot_free[ps_].items()]
                op("vector", (lambda pqb=pqb, qbase=qbase: nc.vector.tensor_copy(
                    bass.AP(phiq_sb[:].tensor, pqb + 1,
                            [[2 * PHIQ_COLS, 128], [PPAD, H], [1, 16]]),
                    bass.AP(qkpad_sb[:].tensor, qbase,
                            [[2 * 2 * QKP, 128], [32, H], [1, 16]]))),
                    waits=[("ve", ve_evac)] + qslot_waits, inc=("ve", 1))
                op("scalar", (lambda pqb=pqb, qbase=qbase: nc.scalar.activation(
                    out=bass.AP(phiq_sb[:].tensor, pqb + 17,
                                [[2 * PHIQ_COLS, 128], [PPAD, H], [1, 16]]),
                    in_=bass.AP(qkpad_sb[:].tensor, qbase,
                                [[2 * 2 * QKP, 128], [32, H], [1, 16]]),
                    func=AF.Square)),
                    waits=[("ve", ve_evac)] + qslot_waits, inc=("ac", 1))
                for gp in range(1, 16):
                    w = 16 - gp
                    eng = "gpsimd" if gp % 2 else "vector"
                    in0 = bass.AP(qkpad_sb[:].tensor, qbase, [[2 * 2 * QKP, 128], [32, H], [1, w]])
                    in1 = bass.AP(qkpad_sb[:].tensor, qbase + gp,
                                  [[2 * 2 * QKP, 128], [32, H], [1, w]])
                    dst = bass.AP(phiq_sb[:].tensor, pqb + GAP_OFF[gp],
                                  [[2 * PHIQ_COLS, 128], [PPAD, H], [1, w]])
                    e = nc.vector if eng == "vector" else nc.gpsimd
                    op(eng, (lambda e=e, dst=dst, in0=in0, in1=in1:
                        e.tensor_tensor(dst, in0, in1, ALU.mult)),
                        waits=[("ve", ve_evac)] + qslot_waits,
                        inc=(("ve" if eng == "vector" else "po"), 1))
                mq = {"ve": mark("ve"), "po": mark("po"), "ac": mark("ac")}
                op("sync", (lambda ps_=ps_, cglob=cglob: nc.sync.dma_start(
                    out=d_phiq[cglob * CH:(cglob + 1) * CH, :],
                    in_=phiq_sb[:, ps_ * PHIQ_COLS:(ps_ + 1) * PHIQ_COLS])),
                    waits=[(s, v) for s, v in mq.items()], inc=("st", 16))
                phiq_slot_free[ps_] = {"st": dma_mark("st")}
                qk_slot_free[qs] = {"st": proj_qk_st["st"], "ve": mark("ve"),
                                    "po": mark("po"), "ac": mark("ac")}
                return marks_k

            def make_update_emitters(gidx, ci, phik_marks):
                cglob_r = (gidx * 4 + ci) % 4
                pkb = cglob_r * PHIK_COLS
                vb = cglob_r * VA
                wmap = {}
                for s, v in [("ve", phik_marks["ve"]), ("po", phik_marks["po"]),
                             ("ac", phik_marks["ac"])]:
                    wmap[s] = max(wmap.get(s, 0), v)
                upd_pe_by_bank = {}
                ems = []
                for n0, (h, c, b, r, flo) in enumerate(PARTS_BY_BANK):
                    bank, col = STATE_LOC[c]
                    idx = [i for i, pp in enumerate(BD[c]) if pp[0] == h][0]
                    scol = col + 65 * idx
                    last_of_bank = (n0 + 1 == len(PARTS_BY_BANK) or
                                    STATE_LOC[PARTS_BY_BANK[n0 + 1][1]][0] != bank)
                    bw = dict(wmap)
                    ss, sv = snap_ack[bank]
                    bw[ss] = max(bw.get(ss, 0), sv)

                    def em(h=h, b=b, r=r, flo=flo, bank=bank, scol=scol,
                           pkb=pkb, vb=vb, last_of_bank=last_of_bank, bw=bw):
                        op("tensor", (lambda:
                            nc.tensor.matmul(
                                PS_STATE[bank][b:b + r, scol:scol + 65],
                                phik_sb[:, pkb + PHI * h + flo:pkb + PHI * h + flo + r],
                                vaug_sb[:, vb + 65 * h:vb + 65 * h + 65],
                                start=False, stop=True, skip_group_check=True)),
                            waits=[(s, v) for s, v in bw.items()],
                            inc=(("pe", 1) if last_of_bank else None))
                        if last_of_bank:
                            upd_pe_by_bank[bank] = mark("pe")
                    ems.append(em)

                def finish():
                    upd_done["pe"] = mark("pe")
                    upd_done["by_bank"] = upd_pe_by_bank
                    vk_slot_free[cglob_r] = {"pe": upd_done["pe"]}
                return ems, finish

            def emit_updates(gidx, ci, phik_marks):
                ems, finish = make_update_emitters(gidx, ci, phik_marks)
                for em in ems:
                    em()
                finish()

            SNAP_ENG = {0: "scalar", 1: "vector", 2: "scalar", 3: "vector",
                        4: "scalar"}

            def emit_snapshot(min_pe=0):
                by_bank = upd_done.get("by_bank", {})
                pv = upd_done["pe"]
                for b in range(N_STATE_BANKS):
                    w = [("pe", max(by_bank.get(b, pv), min_pe))]
                    if SNAP_ENG[b] == "scalar":
                        op("scalar", (lambda b=b: nc.scalar.copy(
                            snap_sb[:, b * 512:(b + 1) * 512],
                            PS_STATE[b][:, 0:512])), waits=w, inc=("ac", 1))
                        snap_ack[b] = ("ac", mark("ac"))
                    else:
                        op("vector", (lambda b=b: nc.vector.tensor_copy(
                            snap_sb[:, b * 512:(b + 1) * 512],
                            PS_STATE[b][:, 0:512])), waits=w, inc=("ve", 1))
                        snap_ack[b] = ("ve", mark("ve"))
                snap_ready["done"] = 1

            xg_marks = issue_x_loads(d_xp, 0)
            upd_pending = []
            go_wot = {"v": 0}
            for gidx in range(NG):
                m = emit_stats_xa(xg_marks)
                xstage_free = {"ve": m["ve"], "po": m["po"], "pe": m["pe"], "ac": m["ac"]}
                if gidx == 1:
                    op("gpsimd", lambda: nc.gpsimd.dma_start(
                        out=wot_sb[:].rearrange("p (c n) -> p c n", n=DM),
                        in_=d_wot.rearrange("(c p) n -> p c n", p=128)),
                        inc=("go", 16))
                    go_wot["v"] = mark("go")
                if gidx < NG - 1:
                    xg_marks = issue_x_loads(d_xp, gidx + 1)
                else:
                    xg_marks = issue_x_loads(d_xt, 0)
                for ci in range(4):
                    uq = None
                    if len(upd_pending) == 2:
                        uq, ufin = upd_pending.pop(0)
                    emit_proj_chunk(gidx, ci, prep=True, upd_q=uq)
                    if uq is not None:
                        for em in uq:
                            em()
                        ufin()
                    mk = emit_phi_chunk(gidx, ci, prep=True)
                    upd_pending.append(make_update_emitters(gidx, ci, mk))
                xstage_free = {"ve": mark("ve"), "po": mark("po"), "pe": mark("pe"),
                               "ac": mark("ac")}

            phiqfm_free = {}
            yT_free = {}

            for gidx in range(NG):
                t0 = GT * gidx
                m = emit_stats_xa(xg_marks)
                xstage_free = {"ve": m["ve"], "po": m["po"], "pe": m["pe"], "ac": m["ac"]}
                phik_marks_by_ci = {}
                fmw = ([(s, v) for s, v in fm32_free.items()]
                       + [(s, v) for s, v in phiqfm_free.items()])
                tr_fm_half = {}
                tr_phiq_half = {}
                for ci in range(4):
                    uq = None
                    if upd_pending:
                        uq, ufin0 = upd_pending.pop(0)
                    emit_proj_chunk(gidx, ci, prep=False, upd_q=uq)
                    if uq is not None:
                        for em in uq:
                            em()
                        ufin0()
                    mk = emit_phi_chunk(gidx, ci, prep=False)
                    phik_marks_by_ci[ci] = mk
                    if ci in (1, 3):
                        half = ci // 2
                        st_c = mark("st")
                        r0 = t0 + half * 256
                        batch_begin("tr")
                        for side in range(2):
                            for t in range(4):
                                src_c0 = side * QKP + FM32_SRC[t]
                                dst_c0 = (side * 4 + t) * GT + half * 256
                                op("sync", (lambda src_c0=src_c0, dst_c0=dst_c0,
                                            r0=r0: nc.sync.dma_start(
                                        out=fm32_sb[:, dst_c0:dst_c0 + 256],
                                        in_=d_qkpad[r0:r0 + 256,
                                                    src_c0:src_c0 + 128],
                                        transpose=True)),
                                    waits=[("st", st_c)] + fmw, inc=("tr", 16))
                        tr_fm_half[half] = batch_end("tr")
                        batch_begin("t2")
                        for j in range(NPC):
                            op("sync", (lambda j=j, r0=r0, half=half:
                                nc.sync.dma_start(
                                    out=phiqfm_sb[:, j * GT + half * 256:
                                                  j * GT + half * 256 + 256],
                                    in_=d_phiq[r0:r0 + 256,
                                               j * 128:(j + 1) * 128],
                                    transpose=True)),
                                waits=[("st", st_c)] + fmw, inc=("t2", 16))
                        tr_phiq_half[half] = batch_end("t2")
                xstage_free = {"ve": mark("ve"), "po": mark("po"), "pe": mark("pe"),
                               "ac": mark("ac")}
                if gidx == 0:
                    emit_snapshot()
                    snap_stage("prep")
                    snap_stage("g0p1")

                prev_pe_y = mark("pe")
                p2_ve, p2_ac, p2_po = mark("ve"), mark("ac"), mark("po")
                yTw = [(s, v) for s, v in yT_free.items()]
                yT_marks = {}
                for ci in range(4):
                    cglob = gidx * 4 + ci
                    vs = cglob % 4
                    vb = vs * VA
                    ywx = [("ve", p2_ve), ("ac", p2_ac), ("po", p2_po)] \
                        if ci == 0 else []
                    for b, w in ((0, 455), (1, 455), (2, 130)):
                        bm = yev_done["b0" if b == 0 else
                                      ("b1" if b == 1 else "b2")] \
                            if "b0" in yev_done else yev_done["ve"]
                        op("tensor", (lambda b=b, w=w: nc.tensor.matmul(
                            PS_G[b][:, 0:w], zt[:, 0:128], zt[:, 0:w],
                            start=True, stop=True, skip_group_check=True)),
                            waits=[("ve", bm)] + ywx, inc=("pe", 1))
                    pe_yzero = mark("pe")
                    if gidx == 0 and ci == 0:
                        snap_stage("c0z")
                    upd_ems, upd_finish = make_update_emitters(
                        gidx, ci, phik_marks_by_ci[ci])
                    upd_q = list(upd_ems)
                    ac_poly = {}
                    ve_mask = {}
                    for h in range(H):
                        tile, base = fm32_loc(h)
                        kb = (4 + tile) * GT + ci * CH
                        qb = tile * GT + ci * CH
                        sc = SCORE_COL[h % 2]
                        w = [("tr", tr_fm_half[ci // 2]),
                             ("ve", yev_done.get("b2", yev_done["ve"]))]
                        if ci == 0 and h == 0:
                            w += [("ve", p2_ve), ("ac", p2_ac)]
                        if h >= 1:
                            w.append(("ac", ac_poly[h - 1]))
                        tp = {"tile_position": (96, 0)} if base == 96 else {}
                        op("tensor", (lambda base=base, kb=kb, qb=qb, sc=sc,
                                      tp=tp: nc.tensor.matmul(
                                ps_g2[:, sc:sc + 128],
                                fm32_sb[base:base + 32, kb:kb + CH],
                                fm32_sb[base:base + 32, qb:qb + CH],
                                start=True, stop=True, skip_group_check=True,
                                **tp)),
                            waits=w, inc=("pe", 1))
                        pe_sc = mark("pe")
                        pw = [("pe", pe_sc)]
                        if h >= 2:
                            pw.append(("ve", ve_mask[h - 2]))
                        op("scalar", (lambda sc=sc, h=h: nc.scalar.activation(
                            out=sqtmp[:, (h % 2) * 128:(h % 2) * 128 + 128],
                            in_=ps_g2[:, sc:sc + 128], func=AF.Square,
                            bias=scb[:, 1:2], scale=scb[:, 0:1])),
                            waits=pw, inc=("ac", 1))
                        ac_poly[h] = mark("ac")
                        op("vector", (lambda h=h: nc.vector.scalar_tensor_tensor(
                            mst_sb[:, h * 128:h * 128 + 128],
                            sqtmp[:, (h % 2) * 128:(h % 2) * 128 + 128], 0.5,
                            mask_sb[:], ALU.add, ALU.mult)),
                            waits=[("ac", ac_poly[h]), ("ld", ld_smalls),
                                   ("pe", prev_pe_y)], inc=("ve", 1))
                        ve_mask[h] = mark("ve")
                        for _ in range(2):
                            if upd_q:
                                upd_q.pop(0)()
                        if gidx == 0 and ci == 0 and h == 0:
                            snap_stage("c0s1")
                        if gidx == 0 and ci == 0 and h == 3:
                            snap_stage("c0s4")
                    if gidx == 0 and ci == 0:
                        snap_stage("c0sc")
                    ac_all_poly = ac_poly[H - 1]
                    for h in range(H):
                        yb, yc = y_loc(h)
                        iw0 = [("ve", ve_mask[h])]
                        if yb == 2:
                            iw0.append(("ac", ac_all_poly))
                        op("tensor", (lambda h=h, yb=yb, yc=yc, vb=vb:
                            nc.tensor.matmul(
                                PS_G[yb][:, yc:yc + 65],
                                mst_sb[:, h * 128:h * 128 + 128],
                                vaug_sb[:, vb + 65 * h:vb + 65 * h + 65],
                                start=False, stop=True, skip_group_check=True)),
                            waits=iw0, inc=None)
                    if gidx == 0 and ci == 0:
                        snap_stage("c0intra")
                    for j in range(NPC):
                        bank, col = STATE_LOC[j]
                        iw = ([snap_ack[bank]] if snap_ready["done"] else []) \
                            + [("t2", tr_phiq_half[ci // 2])]
                        heads = [p[0] for p in BD[j]]
                        h0 = heads[0]
                        width = TILE_W[j]
                        yb, yc = y_loc(h0)
                        segs = []
                        if len(heads) == 2 and y_loc(heads[0])[0] != y_loc(heads[1])[0]:
                            segs = [(heads[0], 0, 65), (heads[1], 65, 65)]
                        else:
                            segs = [(h0, 0, width)]
                        for (hh, so, sw) in segs:
                            yb2, yc2 = y_loc(hh)
                            iw2 = iw + ([("ac", ac_all_poly)] if yb2 == 2 else [])
                            op("tensor", (lambda j=j, bank=bank, col=col, so=so, sw=sw,
                                          yb2=yb2, yc2=yc2, ci=ci:
                                nc.tensor.matmul(
                                    PS_G[yb2][:, yc2:yc2 + sw],
                                    phiqfm_sb[:, j * GT + ci * CH:j * GT + ci * CH + CH],
                                    snap_sb[:, bank * 512 + col + so:bank * 512 + col + so + sw],
                                    start=False, stop=True, skip_group_check=True)),
                                waits=iw2, inc=None)
                    op("tensor", lambda: nc.tensor.matmul(
                        PS_G[0][:, 0:65], zt[:, 0:128], zt[:, 0:65],
                        start=False, stop=True, skip_group_check=True), inc=("pe", 1))
                    pe_y = mark("pe")
                    prev_pe_y = pe_y
                    if gidx == 0 and ci == 0:
                        snap_stage("c0inter")
                    for em in upd_q:
                        em()
                    upd_finish()
                    if gidx == 0 and ci == 0:
                        snap_stage("c0upd")
                    emit_snapshot(min_pe=pe_y)
                    if gidx == 0 and ci == 0:
                        snap_stage("c0snap")
                    ysl = cglob % 2
                    yw2 = [("pe", pe_y)] + [(s, v) for s, v in ydiv_slot_free[ysl].items()]
                    for b, nslots in ((2, 2), (0, 7), (1, 7)):
                        op("vector", (lambda b=b, nslots=nslots: nc.vector.tensor_scalar(
                            rec_sb[:, 24 + 8 * b:24 + 8 * b + nslots],
                            bass.AP(PS_G[b][:].tensor, 64, [[512, 128], [65, nslots]]),
                            EPS_DEN, None, ALU.add)),
                            waits=yw2, inc=("ve", 1))
                        op("vector", (lambda b=b, nslots=nslots: nc.vector.reciprocal(
                            rec_sb[:, 8 * b:8 * b + nslots],
                            rec_sb[:, 24 + 8 * b:24 + 8 * b + nslots])),
                            waits=[("ve", mark("ve"))], inc=("ve", 1))
                        if b == 2:
                            for h in (14, 15):
                                yb, yc = y_loc(h)
                                rcol = 8 * yb + (h - 14)
                                op("vector", (lambda h=h, yb=yb, yc=yc, rcol=rcol, ysl=ysl:
                                    nc.vector.tensor_scalar(
                                        ydiv_sb[:, ysl * DM + 64 * h:ysl * DM + 64 * h + 64],
                                        PS_G[yb][:, yc:yc + 64], rec_sb[:, rcol:rcol + 1],
                                        None, ALU.mult)),
                                    waits=[("ve", mark("ve"))], inc=("ve", 1))
                            yev_done["b2"] = mark("ve")
                    for h in range(14):
                        yb, yc = y_loc(h)
                        rcol = 8 * yb + (h - (0, 7, 14)[yb])
                        op("vector", (lambda h=h, yb=yb, yc=yc, rcol=rcol, ysl=ysl:
                            nc.vector.tensor_scalar(
                                ydiv_sb[:, ysl * DM + 64 * h:ysl * DM + 64 * h + 64],
                                PS_G[yb][:, yc:yc + 64], rec_sb[:, rcol:rcol + 1],
                                None, ALU.mult)),
                            waits=[("ve", mark("ve"))] if h == 0 else (), inc=("ve", 1))
                        if h == 6:
                            yev_done["b0"] = mark("ve")
                    yev_done["b1"] = mark("ve")
                    yev_done["ve"] = mark("ve")
                    op("sync", (lambda ysl=ysl, cglob=cglob: nc.sync.dma_start(
                        out=d_ydiv[cglob * CH:(cglob + 1) * CH, :],
                        in_=ydiv_sb[:, ysl * DM:(ysl + 1) * DM])),
                        waits=[("ve", yev_done["ve"])], inc=("st", 16))
                    ydiv_slot_free[ysl] = {"st": dma_mark("st")}
                    ysem = "tr" if ci % 2 == 0 else "t2"
                    batch_begin(ysem)
                    for j in range(KCN):
                        op("sync", (lambda j=j, cglob=cglob, ci=ci:
                            nc.sync.dma_start(
                                out=yT_sb[:, j * GT + ci * CH:
                                          j * GT + ci * CH + CH],
                                in_=d_ydiv[cglob * CH:(cglob + 1) * CH,
                                           j * 128:(j + 1) * 128],
                                transpose=True)),
                            waits=[("st", ydiv_slot_free[ysl]["st"])] + yTw,
                            inc=(ysem, 16))
                    yT_marks[ci % 2] = batch_end(ysem)
                    if gidx == 0 and ci == 0:
                        snap_stage("c0yev")
                if gidx == 0:
                    snap_stage("g0p2")
                fm32_free = {"pe": mark("pe")}
                phiqfm_free = {"pe": mark("pe")}

                bank_war = {0: yev_done["b0"], 1: yev_done["b1"]}
                for dc in range(KCN):
                    psb = PS_G[dc % 2]
                    for kc in range(KCN):
                        op("tensor", (lambda dc=dc, kc=kc, psb=psb: nc.tensor.matmul(
                            psb[:, 0:GT],
                            wot_sb[:, kc * DM + dc * 128:kc * DM + dc * 128 + 128],
                            yT_sb[:, kc * GT:(kc + 1) * GT],
                            start=(kc == 0), stop=(kc == KCN - 1))),
                            waits=[("tr", yT_marks[0]), ("t2", yT_marks[1]),
                                   ("go", go_wot["v"]), ("ve", bank_war[dc % 2])],
                            inc=(("pe", 1) if kc == KCN - 1 else None))
                    pv = mark("pe")
                    rsl = dc % 4
                    rw = [(s, v) for s, v in res_slot_free[rsl].items()]
                    op("vector", (lambda psb=psb, rsl=rsl, dc=dc: nc.vector.tensor_add(
                        res_sb[:, rsl * GT:(rsl + 1) * GT], psb[:, 0:GT],
                        xg_sb[:, dc * GT:(dc + 1) * GT])),
                        waits=[("pe", pv)] + rw, inc=("ve", 1))
                    ve_res = mark("ve")
                    bank_war[rsl] = ve_res
                    xg_readers[dc]["ve"] = max(xg_readers[dc].get("ve", 0), ve_res)
                    stq = "st" if dc % 2 == 0 else "go"
                    op(DMA_ISSUER[stq], (lambda dc=dc, gidx=gidx, rsl=rsl, stq=stq:
                        (nc.sync if stq == "st" else nc.gpsimd).dma_start(
                            out=d_x2[(gidx * KCN + dc) * 128:
                                     (gidx * KCN + dc + 1) * 128, :],
                            in_=res_sb[:, rsl * GT:(rsl + 1) * GT])),
                        waits=[("ve", ve_res)], inc=(stq, 16))
                    res_slot_free[rsl] = {stq: dma_mark(stq)}
                    if gidx < NG - 1:
                        issue_one_x_load(d_xt, gidx + 1, dc, [])
                yT_free = {"pe": mark("pe")}
                if gidx < NG - 1:
                    xg_marks = {"ld": mark("ld"), "lb": mark("lb")}
                if gidx == 0:
                    snap_stage("g0p3")

            for s_ in ("ld", "lb", "tr", "t2", "go", "st"):
                dma_mark(s_)
            mixer_end_marks.update({s: mark(s) for s in
                                    ("ld", "lb", "tr", "t2", "go", "st", "pe",
                                     "ve", "ac", "po")})

        with contextlib.ExitStack() as _fstk:
            def FS(name, shape, dt):
                return _fstk.enter_context(nc.sbuf_tensor(name, shape, dt))
            ps_f0 = _fstk.enter_context(nc.psum_tensor("ps_f0", [128, 512], F32))
            ps_f1 = _fstk.enter_context(nc.psum_tensor("ps_f1", [128, 512], F32))
            ps_f2 = _fstk.enter_context(nc.psum_tensor("ps_f2", [128, 512], F32))
            ps_f3 = _fstk.enter_context(nc.psum_tensor("ps_f3", [128, 512], F32))
            ps_T = _fstk.enter_context(nc.psum_tensor("ps_T", [32, 128], BF16))
            wi_sb = FS("wi_sb", [128, KCN * FF], BF16)
            wo2_sb = FS("wo2_sb", [128, 32 * DM], BF16)
            xa2_sb = FS("xa2_sb", [128, KCN * 512], BF16)
            inter_sb = FS("inter_sb", [128, 32 * 512], BF16)
            mx_sb = FS("mx_sb", [128, 1], F32)
            oh_sb = FS("oh_sb", [128, 8], F32)
            umb_sb = FS("umb_sb", [128, 32], BF16)
            umT_sb = FS("umT_sb", [64, 512], BF16)
            f2_sb = FS("f2_sb", [128, 4 * 512], F32)
            sqs2_sb = FS("sqs2_sb", [128, 2 * 512], BF16)
            x2f_sb = FS("x2f_sb", [128, 2 * 512], F32)
            abc2_sb = FS("abc2_sb", [128, 512], BF16)
            bbc2_sb = FS("bbc2_sb", [128, 512], BF16)
            onerwf2 = FS("onerwf2", [1, 128], F32)
            outst_sb = FS("out_sb", [128, 2 * 512], F32)
            lbt_sb = FS("lbt_sb", [64, FF // 2], BF16)
            PS_F = [ps_f0, ps_f1, ps_f2, ps_f3]
            barrier = [(s, v) for s, v in mixer_end_marks.items()]
            GT2 = 512
            op("vector", lambda: nc.vector.memset(onerwf2[:], 1.0),
               waits=barrier, inc=("ve", 1))
            op("gpsimd", lambda: nc.gpsimd.dma_start(
                out=lbt_sb[0:32, :].rearrange("r (m c) -> r m c", c=128),
                in_=bass.AP(d_lbt.tensor, 0, [[FF, 32], [256, 16], [1, 128]])),
               waits=barrier, inc=("go", 16))
            op("gpsimd", lambda: nc.gpsimd.dma_start(
                out=lbt_sb[32:64, :].rearrange("r (m c) -> r m c", c=128),
                in_=bass.AP(d_lbt.tensor, 128, [[FF, 32], [256, 16], [1, 128]])),
               inc=("go", 16))
            go_lbt = dma_mark("go")
            x2_pipe_prev = {"ld": mark("ld"), "lb": mark("lb")}
            f2_slot_free = [dict(barrier) for _ in range(4)]

            def issue_f2_load(hgi, k):
                t0 = GT2 * hgi
                s = "ld" if k % 2 == 0 else "lb"
                wl = ([(s, x2_pipe_prev[s])]
                      + [(ss, vv) for ss, vv in f2_slot_free[k % 4].items()])
                op("sync", (lambda k=k, hgi=hgi: nc.sync.dma_start(
                    out=f2_sb[:, (k % 4) * GT2:(k % 4 + 1) * GT2],
                    in_=d_x2[(hgi * KCN + k) * 128:
                             (hgi * KCN + k + 1) * 128, :])),
                    waits=wl, inc=(s, 16))
                x2_pipe_prev[s] = mark(s)
                return (s, x2_pipe_prev[s])

            x2_load_marks = [issue_f2_load(0, k) for k in range(4)]
            op("gpsimd", lambda: nc.gpsimd.dma_start(
                out=wi_sb[:].rearrange("p (c n) -> p c n", n=FF)[:, 0:4, :],
                in_=d_wit.rearrange("(c p) n -> p c n", p=128)[:, 0:4, :]),
                waits=barrier, inc=("go", 16))
            op("gpsimd", lambda: nc.gpsimd.dma_start(
                out=wi_sb[:].rearrange("p (c n) -> p c n", n=FF)[:, 4:8, :],
                in_=d_wit.rearrange("(c p) n -> p c n", p=128)[:, 4:8, :]),
                waits=(), inc=("go", 16))
            ld_w = [("go", mark("go"))]
            op("gpsimd", lambda: nc.gpsimd.dma_start(
                out=wo2_sb[:].rearrange("p (c n) -> p c n", n=DM)[:, 0:16, :],
                in_=d_wot2.rearrange("(c p) n -> p c n", p=128)[:, 0:16, :]),
                waits=(), inc=("go", 16))
            op("gpsimd", lambda: nc.gpsimd.dma_start(
                out=wo2_sb[:].rearrange("p (c n) -> p c n", n=DM)[:, 16:32, :],
                in_=d_wot2.rearrange("(c p) n -> p c n", p=128)[:, 16:32, :]),
                waits=(), inc=("go", 16))
            ld_w2 = [("go", mark("go"))]
            ve_wswi = mark("ve")
            snap_stage("mixer")

            x2stage_free = dict(barrier)
            inter_free = {}
            out_slot_free = [{}, {}]
            relu_done = {}
            wo2_war = {}

            ffn_xa2_free = {"pe": 0}
            ffn_sqs_war = {"pe": 0}

            def emit_ffn_stats(hgi, lm):
                entry_ve, entry_po, entry_ac = mark("ve"), mark("po"), mark("ac")
                xa2_pe = ffn_xa2_free["pe"]
                cast_marks = []
                cast_first = {"scalar": True, "vector": True}
                for k in range(KCN):
                    eng = "vector" if k % 2 == 0 else "scalar"
                    cs = "ve" if eng == "vector" else "ac"
                    if eng == "scalar":
                        fn = (lambda k=k: nc.scalar.copy(
                            xa2_sb[:, k * GT2:(k + 1) * GT2],
                            f2_sb[:, (k % 4) * GT2:(k % 4 + 1) * GT2]))
                    else:
                        fn = (lambda k=k: nc.vector.tensor_copy(
                            xa2_sb[:, k * GT2:(k + 1) * GT2],
                            f2_sb[:, (k % 4) * GT2:(k % 4 + 1) * GT2]))
                    op(eng, fn,
                        waits=[lm[k]]
                              + ([("pe", xa2_pe)] if cast_first[eng] else []),
                        inc=(cs, 1))
                    cast_first[eng] = False
                    cast_marks.append((cs, mark(cs)))
                    op("tensor", (lambda k=k: nc.tensor.matmul(
                        ps_g2[0:1, 0:GT2], onecf[:, 0:1],
                        f2_sb[:, (k % 4) * GT2:(k % 4 + 1) * GT2],
                        start=(k == 0), stop=(k == KCN - 1))),
                        waits=[lm[k]]
                              + ([("ve", max(ve_wswi, entry_ve)),
                                  ("ac", entry_ac), ("po", entry_po)]
                                 if k == 0 else []),
                        inc=("pe", 1))
                    f2_slot_free[k % 4] = {cast_marks[k][0]: cast_marks[k][1],
                                           "pe": mark("pe")}
                    if k + 4 < KCN:
                        lm.append(issue_f2_load(hgi, k + 4))
                sq_pe = {-2: ffn_sqs_war["pe"], -1: ffn_sqs_war["pe"]}
                for k in range(KCN):
                    op("scalar", (lambda k=k: nc.scalar.activation(
                        out=sqs2_sb[:, (k % 2) * GT2:(k % 2 + 1) * GT2],
                        in_=xa2_sb[:, k * GT2:(k + 1) * GT2],
                        func=AF.Square)),
                        waits=[cast_marks[k], ("pe", sq_pe[k - 2])],
                        inc=("ac", 1))
                    sqm = ("ac", mark("ac"))
                    op("tensor", (lambda k=k: nc.tensor.matmul(
                        ps_g0[0:1, 0:GT2], onec[:, 0:1],
                        sqs2_sb[:, (k % 2) * GT2:(k % 2 + 1) * GT2],
                        start=(k == 0), stop=(k == KCN - 1))),
                        waits=[sqm], inc=("pe", 1))
                    sq_pe[k] = mark("pe")
                ffn_sqs_war["pe"] = sq_pe[KCN - 1]
                pv = mark("pe")
                op("vector", lambda: nc.vector.tensor_scalar(
                    srow[0:1, 0:GT2], ps_g2[0:1, 0:GT2], INV_DM, None, ALU.mult),
                    waits=[("pe", pv)], inc=("ve", 1))
                op("vector", lambda: nc.vector.tensor_scalar(
                    srow[0:1, GT:GT + GT2], ps_g0[0:1, 0:GT2], INV_DM, None, ALU.mult),
                    inc=("ve", 1))
                op("vector", lambda: nc.vector.tensor_mul(
                    arow[0:1, 0:GT2], srow[0:1, 0:GT2], srow[0:1, 0:GT2]),
                    waits=[("ve", mark("ve"))], inc=("ve", 1))
                op("vector", lambda: nc.vector.scalar_tensor_tensor(
                    srow[0:1, 2 * GT:2 * GT + GT2], srow[0:1, GT:GT + GT2], EPS_LN,
                    arow[0:1, 0:GT2], ALU.add, ALU.subtract),
                    waits=[("ve", mark("ve"))], inc=("ve", 1))
                vv = mark("ve")
                op("scalar", lambda: nc.scalar.activation(
                    out=srow[0:1, GT:GT + GT2], in_=srow[0:1, 2 * GT:2 * GT + GT2],
                    func=AF.Ln), waits=[("ve", vv)], inc=("ac", 1))
                op("scalar", lambda: nc.scalar.activation(
                    out=arow[0:1, 0:GT2], in_=srow[0:1, GT:GT + GT2],
                    func=AF.Exp, scale=-0.5),
                    waits=[("ac", mark("ac"))], inc=("ac", 1))
                av = mark("ac")
                op("vector", lambda: nc.vector.scalar_tensor_tensor(
                    srow[0:1, 2 * GT:2 * GT + GT2], srow[0:1, 0:GT2], -1.0,
                    arow[0:1, 0:GT2], ALU.mult, ALU.mult),
                    waits=[("ac", av), ("ve", mark("ve"))], inc=("ve", 1))
                vv = mark("ve")
                op("tensor", lambda: nc.tensor.matmul(
                    ps_g0[:, 0:GT2], onerwf2[0:1, :], arow[0:1, 0:GT2],
                    start=True, stop=True), waits=[("ve", vv)], inc=("pe", 1))
                op("tensor", lambda: nc.tensor.matmul(
                    ps_g1[:, 0:GT2], onerwf2[0:1, :], srow[0:1, 2 * GT:2 * GT + GT2],
                    start=True, stop=True), waits=[("ac", entry_ac)], inc=("pe", 1))
                pv = mark("pe")
                op("scalar", lambda: nc.scalar.copy(abc2_sb[:, 0:GT2], ps_g0[:, 0:GT2]),
                    waits=[("pe", pv), ("ve", entry_ve), ("po", entry_po)], inc=("ac", 1))
                op("scalar", lambda: nc.scalar.copy(bbc2_sb[:, 0:GT2], ps_g1[:, 0:GT2]),
                    inc=("ac", 1))
                ac_abc = mark("ac")
                for k in range(KCN):
                    op("vector", (lambda k=k: nc.vector.tensor_mul(
                        xa2_sb[:, k * GT2:(k + 1) * GT2], xa2_sb[:, k * GT2:(k + 1) * GT2],
                        abc2_sb[:, 0:GT2])), waits=[("ac", ac_abc), ("pe", mark("pe"))],
                        inc=("ve", 1))
                    op("vector", (lambda k=k: nc.vector.tensor_add(
                        xa2_sb[:, k * GT2:(k + 1) * GT2], xa2_sb[:, k * GT2:(k + 1) * GT2],
                        bbc2_sb[:, 0:GT2])),
                        waits=[("ve", mark("ve"))], inc=("ve", 1))
                return {"ve": mark("ve"), "po": mark("po")}

            stats_m = emit_ffn_stats(0, x2_load_marks)
            for hgi in range(4):
                t0 = GT2 * hgi
                ve_xa2 = stats_m["ve"]
                po_xa2 = stats_m["po"]

                ve_oh_prev = 0
                pe_T_prev = 0
                ac_T_prev = 0
                for ci in range(4):
                    c0 = ci * CH
                    psg = PS_G[2 if ci % 2 == 0 else 0]
                    for k in range(KCN):
                        op("tensor", (lambda k=k, c0=c0, psg=psg: nc.tensor.matmul(
                            psg[:, 0:8],
                            xa2_sb[:, k * GT2 + c0:k * GT2 + c0 + CH],
                            wgt_sb[:, k * 8:(k + 1) * 8],
                            start=(k == 0), stop=False)),
                            waits=[("ve", max(ve_xa2, ve_oh_prev)), ("po", po_xa2),
                                   ("go", go_smalls)],
                            inc=None)
                    op("tensor", (lambda c0=c0, psg=psg: nc.tensor.matmul(
                        psg[:, 0:8], onerow[0:1, 0:CH], gbr_sb[0:1, :],
                        start=False, stop=True)), inc=None)
                    for k in range(KCN):
                        op("tensor", (lambda k=k, c0=c0, psg=psg: nc.tensor.matmul(
                            psg[:, 8:40],
                            xa2_sb[:, k * GT2 + c0:k * GT2 + c0 + CH],
                            lat_sb[:, k * 32:(k + 1) * 32],
                            start=(k == 0), stop=(k == KCN - 1))),
                            inc=(("pe", 1) if k == KCN - 1 else None))
                    pv = mark("pe")
                    op("vector", (lambda psg=psg: nc.vector.reduce_max(
                        mx_sb[:, 0:1], psg[:, 0:8], axis=AXX)),
                        waits=[("pe", pv)], inc=("ve", 1))
                    op("vector", (lambda psg=psg: nc.vector.tensor_scalar(
                        oh_sb[:], psg[:, 0:8], mx_sb[:, 0:1], None, ALU.is_ge)),
                        waits=[("ve", mark("ve"))], inc=("ve", 1))
                    ohb = bass.AP(oh_sb[:].tensor, 0, [[8, 128], [1, 8], [0, 4]])
                    op("vector", (lambda ohb=ohb, psg=psg: nc.vector.tensor_tensor(
                        umb_sb[:].rearrange("p (e r) -> p e r", r=4),
                        psg[:, 8:40].rearrange("p (e r) -> p e r", r=4),
                        ohb, ALU.mult)),
                        waits=[("pe", pe_T_prev), ("ve", mark("ve"))], inc=("ve", 1))
                    ve_oh_prev = mark("ve")
                    op("tensor", (lambda: nc.tensor.transpose(
                        ps_T[:, :], umb_sb[:], ident_sb[:])),
                        waits=[("ve", ve_oh_prev), ("ac", ac_T_prev), ("go", go_smalls)],
                        inc=("pe", 1))
                    pe_T_prev = mark("pe")
                    op("scalar", (lambda ci=ci: nc.scalar.copy(
                        umT_sb[0:32, ci * CH:(ci + 1) * CH], ps_T[:, :])),
                        waits=[("pe", pe_T_prev)], inc=("ac", 1))
                    op("scalar", (lambda ci=ci: nc.scalar.copy(
                        umT_sb[32:64, ci * CH:(ci + 1) * CH], ps_T[:, :])),
                        inc=("ac", 1))
                    ac_T_prev = mark("ac")
                ac_umT = mark("ac")

                relu_mark_by_m = {}
                intw = [(s, v) for s, v in inter_free.items()]
                for m_ in range(32):
                    psb = PS_F[m_ % 2]
                    for k in range(KCN):
                        op("tensor", (lambda k=k, m_=m_, psb=psb: nc.tensor.matmul(
                            psb[:, 0:GT2],
                            wi_sb[:, k * FF + m_ * 128:k * FF + m_ * 128 + 128],
                            xa2_sb[:, k * GT2:(k + 1) * GT2],
                            start=(k == 0), stop=False)),
                            waits=ld_w + [("ve", ve_xa2), ("po", po_xa2),
                                          ("ac", relu_done.get(m_ % 2, 0))],
                            inc=None)
                    op("tensor", (lambda m_=m_, psb=psb: nc.tensor.matmul(
                        psb[:, 0:GT2],
                        lbt_sb[32 * (m_ % 2):32 * (m_ % 2) + 32,
                               (m_ // 2) * 128:(m_ // 2) * 128 + 128],
                        umT_sb[32 * (m_ % 2):32 * (m_ % 2) + 32, :],
                        start=False, stop=True)),
                        waits=[("ac", ac_umT)], inc=("pe", 1))
                    pv = mark("pe")
                    op("scalar", (lambda m_=m_, psb=psb: nc.scalar.activation(
                        out=inter_sb[:, m_ * GT2:(m_ + 1) * GT2], in_=psb[:, 0:GT2],
                        func=AF.Relu, bias=wib_sb[:, m_:m_ + 1])),
                        waits=[("pe", pv)] + intw, inc=("ac", 1))
                    relu_done[m_ % 2] = mark("ac")
                    relu_mark_by_m[m_] = mark("ac")
                ac_inter = mark("ac")
                ffn_xa2_free["pe"] = mark("pe")
                if hgi < 3:
                    x2_load_marks = [issue_f2_load(hgi + 1, k) for k in range(4)]

                x2f_war = [{}, {}]
                for dc in range(KCN):
                    psb = PS_F[2 + dc % 2]
                    s = "ld" if dc % 2 == 0 else "lb"
                    xw = ([(s, x2_pipe_prev[s])]
                          + [(ss, vv) for ss, vv in x2f_war[dc % 2].items()])
                    op("sync", (lambda dc=dc, hgi=hgi: nc.sync.dma_start(
                        out=x2f_sb[:, (dc % 2) * GT2:(dc % 2 + 1) * GT2],
                        in_=d_x2[(hgi * KCN + dc) * 128:
                                 (hgi * KCN + dc + 1) * 128, :])),
                        waits=xw, inc=(s, 16))
                    x2_pipe_prev[s] = mark(s)
                    ld_xc = (s, x2_pipe_prev[s])
                    for k2 in range(32):
                        op("tensor", (lambda k2=k2, dc=dc, psb=psb: nc.tensor.matmul(
                            psb[:, 0:GT2],
                            wo2_sb[:, k2 * DM + dc * 128:k2 * DM + dc * 128 + 128],
                            inter_sb[:, k2 * GT2:(k2 + 1) * GT2],
                            start=(k2 == 0), stop=(k2 == 31))),
                            waits=[("ac", relu_mark_by_m[k2]),
                                   ("ve", wo2_war.get(dc % 2, 0))] + ld_w2,
                            inc=(("pe", 1) if k2 == 31 else None))
                    pv = mark("pe")
                    osl = dc % 2
                    ow = [(s2, v) for s2, v in out_slot_free[osl].items()]
                    op("vector", (lambda dc=dc, psb=psb, osl=osl:
                        nc.vector.scalar_tensor_tensor(
                            outst_sb[:, osl * GT2:(osl + 1) * GT2], psb[:, 0:GT2],
                            wob_sb[:, dc:dc + 1], x2f_sb[:, osl * GT2:(osl + 1) * GT2],
                            ALU.add, ALU.add)),
                        waits=[("pe", pv), ld_xc] + ow, inc=("ve", 1))
                    wo2_war[dc % 2] = mark("ve")
                    vv = mark("ve")
                    x2f_war[dc % 2] = {"ve": vv}
                    stq = "st" if dc % 2 == 0 else "go"
                    op(DMA_ISSUER[stq], (lambda dc=dc, hgi=hgi, osl=osl, stq=stq:
                        (nc.sync if stq == "st" else nc.gpsimd).dma_start(
                            out=d_out[(hgi * KCN + dc) * 128:
                                      (hgi * KCN + dc + 1) * 128, :],
                            in_=outst_sb[:, osl * GT2:(osl + 1) * GT2])),
                        waits=[("ve", vv)], inc=(stq, 16))
                    out_slot_free[osl] = {stq: dma_mark(stq)}
                    if dc == 1 and hgi < 3:
                        stats_m = emit_ffn_stats(hgi + 1, x2_load_marks)
                inter_free = {"pe": mark("pe")}
                x2stage_free = {"pe": mark("pe"), "ac": mark("ac"), "ve": mark("ve"),
                                "po": mark("po")}
                if hgi == 0:
                    snap_stage("ffn1")

            op("sync", None, waits=[("st", mark("st"))])

        if stage is not None:
            cut = stage_snaps[stage]
            for e in ENGS:
                ops[e] = ops[e][:cut[e]]

        def emit_stream(eng_obj, name):
            for item in ops[name]:
                if item[0] == "wait":
                    eng_obj.wait_ge(SEM[item[1]], item[2])
                else:
                    _, fn, s, amt = item
                    ins = fn()
                    if s is not None:
                        ins.then_inc(SEM[s], amt)

        @block.sync
        def _(sync):
            emit_stream(nc.sync, "sync")

        @block.tensor
        def _(tensor):
            emit_stream(nc.tensor, "tensor")

        @block.vector
        def _(vector):
            emit_stream(nc.vector, "vector")

        @block.scalar
        def _(scalar):
            emit_stream(nc.scalar, "scalar")

        @block.gpsimd
        def _(gpsimd):
            emit_stream(nc.gpsimd, "gpsimd")

    return nc


def kernel(**inputs):
    f32 = np.float32
    x = np.ascontiguousarray(np.asarray(inputs["L0_attn_hidden_states"], f32))
    Wq = np.asarray(inputs["L0_attn_Wq"], f32)
    Wk = np.asarray(inputs["L0_attn_Wk"], f32)
    Wv = np.asarray(inputs["L0_attn_Wv"], f32)
    Wo = np.asarray(inputs["L0_attn_Wo"], f32)
    gW = np.asarray(inputs["L0_ffn_gate_W"], f32)
    gb = np.asarray(inputs["L0_ffn_gate_b"], f32)
    wiW = np.asarray(inputs["L0_ffn_wi_W"], f32)
    wib = np.asarray(inputs["L0_ffn_wi_b"], f32)
    woW = np.asarray(inputs["L0_ffn_wo_W"], f32)
    wob = np.asarray(inputs["L0_ffn_wo_b"], f32)
    lA = np.asarray(inputs["L0_ffn_lora_A"], f32)
    lB = np.asarray(inputs["L0_ffn_lora_B"], f32)

    wqkv = np.zeros((DM, WQKV_COLS), f32)
    for h in range(H):
        wqkv[:, 32 * h:32 * h + 16] = Wq[16 * h:16 * h + 16, :].T
        wqkv[:, QKP + 32 * h:QKP + 32 * h + 16] = Wk[16 * h:16 * h + 16, :].T
        wqkv[:, 2 * QKP + 65 * h:2 * QKP + 65 * h + 64] = Wv[64 * h:64 * h + 64, :].T
    wot = np.ascontiguousarray(Wo.T)
    wit = np.ascontiguousarray(wiW.T)
    wot2 = np.ascontiguousarray(woW.T)
    wgt = np.ascontiguousarray(gW.T)
    lat = np.ascontiguousarray(lA.reshape(32, DM).T)
    lbt = np.ascontiguousarray(lB.transpose(0, 2, 1).reshape(32, FF))
    wibt = np.ascontiguousarray(wib.reshape(32, 128).T)
    wobt = np.ascontiguousarray(wob.reshape(8, 128).T)
    gbr = np.ascontiguousarray(gb[None, :])
    maskt = np.ascontiguousarray(np.triu(np.ones((128, 128), f32)))
    identt = np.ascontiguousarray(np.eye(128, dtype=f32))

    def tile_fm(xt):
        return np.ascontiguousarray(
            xt.reshape(KCN, 128, NG, GT).transpose(2, 0, 1, 3)
              .reshape(NG * KCN * 128, GT))

    nc = build_nc()
    in_maps = []
    for c in range(8):
        seq, half = c // 2, c % 2
        xt = x[seq, half * TOK:(half + 1) * TOK, :].T
        xp = x[seq, 0:TOK, :].T if half == 1 else xt
        xt_t = tile_fm(xt)
        xp_t = tile_fm(xp) if half == 1 else xt_t
        gam = np.full((128, 1), 1.0 if half == 1 else 0.0, f32)
        in_maps.append({
            "xt": xt_t, "xp": xp_t, "gam": gam, "wqkv": wqkv, "wot": wot,
            "wit": wit, "wot2": wot2, "wgt": wgt, "lat": lat, "lbt": lbt,
            "wib": wibt, "wob": wobt, "gbr": gbr, "maskt": maskt,
            "identt": identt,
        })
    res = run_bass_kernel_spmd(nc, in_maps, core_ids=list(range(8)))
    out = np.empty((4, 4096, DM), f32)
    for c in range(8):
        seq, half = c // 2, c % 2
        ot = np.asarray(res.results[c]["outt"]).reshape(NG, KCN, 128, GT)
        ot = ot.transpose(1, 2, 0, 3).reshape(DM, TOK)
        out[seq, half * TOK:(half + 1) * TOK, :] = ot.T
    return out
